# version 04 (final)
# baseline (speedup 1.0000x reference)
import contextlib
import numpy as np
import ml_dtypes
import concourse.bass as bass
import concourse.mybir as mybir
from concourse.bass_utils import run_bass_kernel_spmd

F32 = mybir.dt.float32
BF16 = mybir.dt.bfloat16
AF = mybir.ActivationFunctionType
ALU = mybir.AluOpType
BF = ml_dtypes.bfloat16

B, S, E = 2, 8192, 2048
H, D = 16, 128
HID = 8192
NCORE = 8
TPC = 2048
BHPC = 4
LN_EPS = 1e-5
RMS_EPS = float(np.finfo(np.float32).eps)
ROPE_BASE = 10000.0

ENGINES = ("tensor", "vector", "scalar", "gpsimd", "sync")


class Tok:
    __slots__ = ("name", "writer", "readers")

    def __init__(self, name="t"):
        self.name = name
        self.writer = None
        self.readers = []


class Prog:

    def __init__(self, nc, st, paranoid_same_engine=True):
        self.nc = nc
        self.st = st
        self.paranoid = paranoid_same_engine
        self.sems = {}
        self.counts = {}
        self.clocks = {e: {} for e in ENGINES}
        for e in ("tensor", "vector", "scalar", "gpsimd"):
            self.sems[e] = st.enter_context(nc.semaphore(f"sem_{e}"))
            self.counts[e] = 0
        self.streams = {}
        self.ops = {e: [] for e in ENGINES}
        self.n_waits = 0
        self.n_ops = 0

    def stream(self, name, queue):
        assert name not in self.streams and name not in self.sems
        sem = self.st.enter_context(self.nc.semaphore(f"dma_{name}"))
        self.sems[name] = sem
        self.counts[name] = 0
        self.streams[name] = queue
        return name

    def _queue_wait(self, engine_name, semname, count):
        if count <= 0:
            return
        if self.clocks[engine_name].get(semname, 0) >= count:
            return
        sem = self.sems[semname]
        self.ops[engine_name].append(lambda eng, s=sem, c=count: eng.wait_ge(s, c))
        self.clocks[engine_name][semname] = count
        self.n_waits += 1

    def _edges(self, engine_name, reads, writes):
        need = {}
        for t in reads:
            if t.writer is not None:
                s, c = t.writer
                need[s] = max(need.get(s, 0), c)
        for t in writes:
            if t.writer is not None:
                s, c = t.writer
                need[s] = max(need.get(s, 0), c)
            for s, c in t.readers:
                need[s] = max(need.get(s, 0), c)
        for s, c in sorted(need.items()):
            if s == engine_name and (not self.paranoid or engine_name == "tensor"):
                continue
            self._queue_wait(engine_name, s, c)

    def _commit(self, engine_name, semname, inc, reads, writes):
        self.counts[semname] += inc
        c = self.counts[semname]
        for t in reads:
            t.readers.append((semname, c))
        for t in writes:
            t.writer = (semname, c)
            t.readers = []
        if semname in self.streams:
            return
        if not (self.paranoid and semname == engine_name):
            self.clocks[engine_name][semname] = max(
                self.clocks[engine_name].get(semname, 0), c)

    def emit(self, engine_name, fn, reads=(), writes=()):
        self._edges(engine_name, reads, writes)
        sem = self.sems[engine_name]
        self.ops[engine_name].append(lambda eng, f=fn, s=sem: f(eng).then_inc(s, 1))
        self._commit(engine_name, engine_name, 1, reads, writes)
        self.n_ops += 1

    def dma(self, stream, out, in_, reads=(), writes=(), **kw):
        queue = self.streams[stream]
        self._queue_wait(queue, stream, self.counts[stream])
        self._edges(queue, reads, writes)
        sem = self.sems[stream]
        self.ops[queue].append(
            lambda eng, o=out, i=in_, s=sem, kw=kw: eng.dma_start(out=o, in_=i, **kw).then_inc(s, 16))
        self._commit(queue, stream, 16, reads, writes)
        self.n_ops += 1

    def final_wait(self, engine_name="sync"):
        for s, c in sorted(self.counts.items()):
            if c > 0:
                self._queue_wait(engine_name, s, c)

    def build(self, block):
        def mk(items):
            def body(eng):
                for f in items:
                    f(eng)
            return body
        for e in ENGINES:
            if self.ops[e]:
                getattr(block, e)(mk(self.ops[e]))



def emit_layernorm_chunk_stats(p, ps_mean, ps_e2, t_psm, t_pse2, x_bf, x2_bf,
                               t_xbf, t_x2bf, onesm, t_onesm, nchunks):
    NT = ps_mean.shape[-1]
    for c in range(nchunks):
        p.emit("tensor",
               lambda e, c=c: e.matmul(ps_mean[:], onesm[:], x_bf[:, c, :],
                                       start=(c == 0), stop=(c == nchunks - 1)),
               reads=[t_xbf, t_onesm], writes=[t_psm])
        p.emit("tensor",
               lambda e, c=c: e.matmul(ps_e2[:], onesm[:], x2_bf[:, c, :],
                                       start=(c == 0), stop=(c == nchunks - 1)),
               reads=[t_x2bf, t_onesm], writes=[t_pse2])


def emit_rstd(p, r_sb, t_r, ps_mean, ps_e2, t_psm, t_pse2, m2_sb, t_m2,
              var_sb, t_var, eps_col, t_eps):
    p.emit("scalar", lambda e: e.activation(m2_sb[:], ps_mean[:], AF.Square),
           reads=[t_psm], writes=[t_m2])
    p.emit("vector", lambda e: e.scalar_tensor_tensor(
        var_sb[:], m2_sb[:], -1.0, ps_e2[:], ALU.mult, ALU.add),
        reads=[t_m2, t_pse2], writes=[t_var])
    p.emit("scalar", lambda e: e.activation(var_sb[:], var_sb[:], AF.Ln,
                                            bias=eps_col[:]),
           reads=[t_var, t_eps], writes=[t_var])
    p.emit("scalar", lambda e: e.activation(r_sb[:], var_sb[:], AF.Exp, scale=-0.5),
           reads=[t_var], writes=[t_r])



def build_launch_A():
    nc = bass.Bass("TRN2", target_bir_lowering=False)
    xT = nc.dram_tensor("xT", [E, TPC], F32, kind="ExternalInput").ap()
    w_qkv = nc.dram_tensor("w_qkv", [E, 3 * E], F32, kind="ExternalInput").ap()
    xnT_out = nc.dram_tensor("xnT", [E, TPC], BF16, kind="ExternalOutput").ap()
    qkvT_out = nc.dram_tensor("qkvT", [3 * E, TPC], BF16, kind="ExternalOutput").ap()

    EC = E // 128
    FC = 3 * E // 128
    NTT = TPC // 512

    st = contextlib.ExitStack()
    onesm = st.enter_context(nc.sbuf_tensor("onesm", [128, 128], BF16))
    eps_col = st.enter_context(nc.sbuf_tensor("eps_col", [128, 1], F32))
    x_f = st.enter_context(nc.sbuf_tensor("x_f", [128, EC, 512], F32))
    x_bf = st.enter_context(nc.sbuf_tensor("x_bf", [128, EC, 512], BF16))
    x2_bf = st.enter_context(nc.sbuf_tensor("x2_bf", [128, EC, 512], BF16))
    m2_sb = st.enter_context(nc.sbuf_tensor("m2_sb", [128, 512], F32))
    var_sb = st.enter_context(nc.sbuf_tensor("var_sb", [128, 512], F32))
    r_sb = st.enter_context(nc.sbuf_tensor("r_sb", [128, 512], F32))
    xm_sb = st.enter_context(nc.sbuf_tensor("xm_sb", [128, 512], F32))
    xn_bf = st.enter_context(nc.sbuf_tensor("xn_bf", [128, EC, NTT, 512], BF16))
    wq = [st.enter_context(nc.sbuf_tensor(f"wq{i}", [128, EC, 128], BF16)) for i in range(2)]
    out_bf = [st.enter_context(nc.sbuf_tensor(f"out_bf{i}", [128, 512], BF16)) for i in range(2)]
    ps_mean = st.enter_context(nc.psum_tensor("ps_mean", [128, 512], F32))
    ps_e2 = st.enter_context(nc.psum_tensor("ps_e2", [128, 512], F32))
    ps_qkv = [st.enter_context(nc.psum_tensor(f"ps_qkv{i}", [128, 512], F32)) for i in range(2)]

    p = Prog(nc, st)
    p.stream("xload", "sync")
    p.stream("wload0", "gpsimd")
    p.stream("wload1", "gpsimd")
    p.stream("cload", "gpsimd")
    p.stream("store", "scalar")

    t_onesm, t_eps = Tok("onesm"), Tok("eps")
    t_xf, t_xbf, t_x2bf = Tok("x_f"), Tok("x_bf"), Tok("x2_bf")
    t_psm, t_pse2 = Tok("ps_mean"), Tok("ps_e2")
    t_m2, t_var, t_r, t_xm = Tok("m2"), Tok("var"), Tok("r"), Tok("xm")
    t_xn = Tok("xn_bf")
    t_wq = [Tok("wq0"), Tok("wq1")]
    t_out = [Tok("out0"), Tok("out1")]
    t_psq = [Tok("psq0"), Tok("psq1")]

    p.emit("vector", lambda e: e.memset(onesm[:], 1.0 / E), writes=[t_onesm])
    p.emit("vector", lambda e: e.memset(eps_col[:], LN_EPS), writes=[t_eps])

    for T in range(NTT):
        for c in range(EC):
            p.dma("xload", x_f[:, c, :], xT[c * 128:(c + 1) * 128, T * 512:(T + 1) * 512],
                  writes=[t_xf])
            p.emit("scalar", lambda e, c=c: e.copy(x_bf[:, c, :], x_f[:, c, :]),
                   reads=[t_xf], writes=[t_xbf])
            p.emit("scalar", lambda e, c=c: e.activation(x2_bf[:, c, :], x_f[:, c, :], AF.Square),
                   reads=[t_xf], writes=[t_x2bf])
        emit_layernorm_chunk_stats(p, ps_mean, ps_e2, t_psm, t_pse2, x_bf, x2_bf,
                                   t_xbf, t_x2bf, onesm, t_onesm, EC)
        emit_rstd(p, r_sb, t_r, ps_mean, ps_e2, t_psm, t_pse2, m2_sb, t_m2,
                  var_sb, t_var, eps_col, t_eps)
        for c in range(EC):
            p.emit("vector", lambda e, c=c: e.tensor_tensor(
                xm_sb[:], x_f[:, c, :], ps_mean[:], ALU.subtract),
                reads=[t_xf, t_psm], writes=[t_xm])
            p.emit("vector", lambda e, c=c, T=T: e.tensor_tensor(
                xn_bf[:, c, T, :], xm_sb[:], r_sb[:], ALU.mult),
                reads=[t_xm, t_r], writes=[t_xn])
        for c in range(EC):
            p.dma("store", xnT_out[c * 128:(c + 1) * 128, T * 512:(T + 1) * 512],
                  xn_bf[:, c, T, :], reads=[t_xn])

    for f in range(FC):
        b = f % 2
        p.dma(f"wload{b}", wq[b][:],
              w_qkv[:, f * 128:(f + 1) * 128].rearrange("(c p) f -> p c f", p=128),
              writes=[t_wq[b]])
        for T in range(NTT):
            pb = (f * NTT + T) % 2
            for c in range(EC):
                p.emit("tensor", lambda e, b=b, c=c, T=T, pb=pb: e.matmul(
                    ps_qkv[pb][:], wq[b][:, c, :], xn_bf[:, c, T, :],
                    start=(c == 0), stop=(c == EC - 1)),
                    reads=[t_wq[b], t_xn], writes=[t_psq[pb]])
            p.emit("vector", lambda e, pb=pb: e.tensor_copy(out_bf[pb][:], ps_qkv[pb][:]),
                   reads=[t_psq[pb]], writes=[t_out[pb]])
            p.dma("store", qkvT_out[f * 128:(f + 1) * 128, T * 512:(T + 1) * 512],
                  out_bf[pb][:], reads=[t_out[pb]])

    p.final_wait("sync")
    block = st.enter_context(nc.Block())
    p.build(block)
    st.close()
    return nc



def build_launch_B(seq=S, nbh=BHPC):
    nc = bass.Bass("TRN2", target_bir_lowering=False)
    NT = seq // 512
    qT_in = nc.dram_tensor("qT", [nbh, 128, seq], BF16, kind="ExternalInput").ap()
    kT_in = nc.dram_tensor("kT", [nbh, 128, seq], BF16, kind="ExternalInput").ap()
    v_in = nc.dram_tensor("v", [nbh, seq, 128], BF16, kind="ExternalInput").ap()
    lnsc = nc.dram_tensor("lnsc", [128, 1], F32, kind="ExternalInput").ap()
    lnbi = nc.dram_tensor("lnbi", [128, 1], F32, kind="ExternalInput").ap()
    masks = nc.dram_tensor("masks", [4, 128, 512], BF16, kind="ExternalInput").ap()
    o_out = nc.dram_tensor("o", [nbh, 128, seq], BF16, kind="ExternalOutput").ap()

    st = contextlib.ExitStack()
    cos_t = st.enter_context(nc.sbuf_tensor("cos_t", [128, seq], F32))
    sin_t = st.enter_context(nc.sbuf_tensor("sin_t", [128, seq], F32))
    posi = st.enter_context(nc.sbuf_tensor("posi", [128, 512], mybir.dt.int32))
    pos = st.enter_context(nc.sbuf_tensor("pos", [128, 512], F32))
    pidxi = st.enter_context(nc.sbuf_tensor("pidxi", [128, 1], mybir.dt.int32))
    pidx = st.enter_context(nc.sbuf_tensor("pidx", [128, 1], F32))
    invf = st.enter_context(nc.sbuf_tensor("invf", [128, 1], F32))
    tb = st.enter_context(nc.sbuf_tensor("tb", [128, 512], F32))
    kf = st.enter_context(nc.sbuf_tensor("kf", [128, 512], F32))
    ki = st.enter_context(nc.sbuf_tensor("ki", [128, 512], mybir.dt.int32))
    halfpi = st.enter_context(nc.sbuf_tensor("halfpi", [128, 1], F32))
    sc_col = st.enter_context(nc.sbuf_tensor("sc_col", [128, 1], F32))
    bi_col = st.enter_context(nc.sbuf_tensor("bi_col", [128, 1], F32))
    eps_col = st.enter_context(nc.sbuf_tensor("eps_col", [128, 1], F32))
    onesm = st.enter_context(nc.sbuf_tensor("onesm", [128, 128], BF16))
    onesf = st.enter_context(nc.sbuf_tensor("onesf", [128, 128], F32))
    mask_sb = st.enter_context(nc.sbuf_tensor("mask_sb", [128, 4, 512], BF16))
    qraw = st.enter_context(nc.sbuf_tensor("qraw", [128, 512], BF16))
    qshift = st.enter_context(nc.sbuf_tensor("qshift", [128, 512], BF16))
    Qb = st.enter_context(nc.sbuf_tensor("Qb", [128, seq], BF16))
    Kb = st.enter_context(nc.sbuf_tensor("Kb", [128, seq], BF16))
    Vb = st.enter_context(nc.sbuf_tensor("Vb", [128, seq // 128, 128], BF16))
    Ob = st.enter_context(nc.sbuf_tensor("Ob", [128, seq], BF16))
    rw = [st.enter_context(nc.sbuf_tensor(f"rw{i}", [128, 512], F32)) for i in range(3)]
    q_bf = st.enter_context(nc.sbuf_tensor("q_bf", [128, 512], BF16))
    q2_bf = st.enter_context(nc.sbuf_tensor("q2_bf", [128, 512], BF16))
    m2_sb = st.enter_context(nc.sbuf_tensor("m2_sb", [128, 512], F32))
    var_sb = st.enter_context(nc.sbuf_tensor("var_sb", [128, 512], F32))
    r_sb = st.enter_context(nc.sbuf_tensor("r_sb", [128, 512], F32))
    S2 = [st.enter_context(nc.sbuf_tensor(f"S2_{i}", [128, 512], F32)) for i in range(2)]
    A_bf = [st.enter_context(nc.sbuf_tensor(f"A_bf{i}", [128, 512], BF16)) for i in range(2)]
    dacc = st.enter_context(nc.sbuf_tensor("dacc", [128, 512], F32))
    rden = st.enter_context(nc.sbuf_tensor("rden", [128, 512], F32))
    ps_S = [st.enter_context(nc.psum_tensor(f"ps_S{i}", [128, 512], F32)) for i in range(2)]
    ps_O = [st.enter_context(nc.psum_tensor(f"ps_O{i}", [128, 512], F32)) for i in range(2)]
    ps_mean = st.enter_context(nc.psum_tensor("ps_mean", [128, 512], F32))
    ps_e2 = st.enter_context(nc.psum_tensor("ps_e2", [128, 512], F32))
    ps_den = st.enter_context(nc.psum_tensor("ps_den", [128, 512], F32))

    p = Prog(nc, st)
    p.stream("qload", "sync")
    p.stream("vload", "sync")
    p.stream("cload", "gpsimd")
    p.stream("store", "scalar")

    T = {name: Tok(name) for name in
         ["posi", "pos", "pidxi", "pidx", "invf", "tb", "kf", "ki", "cos", "sin",
          "halfpi", "sc", "bi", "eps", "onesm", "onesf", "mask", "qraw", "qshift",
          "Qb", "Kb", "Vb", "Ob", "rw0", "rw1", "rw2", "q_bf", "q2_bf", "m2",
          "var", "r", "S2_0", "S2_1", "A0", "A1", "dacc", "rden",
          "ps_S0", "ps_S1", "ps_O0", "ps_O1", "ps_mean", "ps_e2", "ps_den"]}

    p.dma("cload", sc_col[:], lnsc[:, :], writes=[T["sc"]])
    p.dma("cload", bi_col[:], lnbi[:, :], writes=[T["bi"]])
    p.dma("cload", mask_sb[:], masks.rearrange("k p f -> p k f"), writes=[T["mask"]])
    p.emit("vector", lambda e: e.memset(eps_col[:], LN_EPS), writes=[T["eps"]])
    p.emit("vector", lambda e: e.memset(onesm[:], 1.0 / 128.0), writes=[T["onesm"]])
    p.emit("vector", lambda e: e.memset(onesf[:], 1.0), writes=[T["onesf"]])
    p.emit("vector", lambda e: e.memset(halfpi[:], float(np.pi / 2)), writes=[T["halfpi"]])

    p.emit("gpsimd", lambda e: e.iota(pidxi[:], pattern=[[0, 1]], base=0,
                                      channel_multiplier=1), writes=[T["pidxi"]])
    p.emit("vector", lambda e: e.tensor_copy(pidx[:], pidxi[:]), reads=[T["pidxi"]], writes=[T["pidx"]])
    p.emit("vector", lambda e: e.tensor_scalar(halfpi[:], pidx[:], 64.0, None, ALU.is_ge),
           reads=[T["pidx"]], writes=[T["halfpi"]])
    p.emit("vector", lambda e: e.scalar_tensor_tensor(pidx[:], halfpi[:], -64.0, pidx[:],
                                                      ALU.mult, ALU.add),
           reads=[T["halfpi"], T["pidx"]], writes=[T["pidx"]])
    p.emit("vector", lambda e: e.memset(halfpi[:], float(np.pi / 2)), writes=[T["halfpi"]])
    p.emit("scalar", lambda e: e.activation(invf[:], pidx[:], AF.Exp,
                                            scale=float(-2.0 * np.log(ROPE_BASE) / D)),
           reads=[T["pidx"]], writes=[T["invf"]])
    TWO_PI = float(2 * np.pi)
    c1 = 6.28125
    c2 = 0.001922607421875
    c3 = float(TWO_PI - c1 - c2)
    for ch in range(NT):
        p.emit("gpsimd", lambda e, ch=ch: e.iota(posi[:], pattern=[[1, 512]],
                                                 base=ch * 512, channel_multiplier=0),
               writes=[T["posi"]])
        p.emit("vector", lambda e: e.tensor_copy(pos[:], posi[:]),
               reads=[T["posi"]], writes=[T["pos"]])
        p.emit("vector", lambda e: e.tensor_scalar(tb[:], pos[:], invf[:], None, ALU.mult),
               reads=[T["pos"], T["invf"]], writes=[T["tb"]])
        p.emit("vector", lambda e: e.tensor_scalar(kf[:], tb[:], 1.0 / TWO_PI, 0.5,
                                                   ALU.mult, ALU.add),
               reads=[T["tb"]], writes=[T["kf"]])
        p.emit("vector", lambda e: e.tensor_copy(ki[:], kf[:]), reads=[T["kf"]], writes=[T["ki"]])
        p.emit("vector", lambda e: e.tensor_copy(kf[:], ki[:]), reads=[T["ki"]], writes=[T["kf"]])
        for cc in (c1, c2, c3):
            p.emit("vector", lambda e, cc=cc: e.tensor_scalar(pos[:], kf[:], float(cc), None, ALU.mult),
                   reads=[T["kf"]], writes=[T["pos"]])
            p.emit("vector", lambda e: e.tensor_tensor(tb[:], tb[:], pos[:], ALU.subtract),
                   reads=[T["tb"], T["pos"]], writes=[T["tb"]])
        sl = slice(ch * 512, (ch + 1) * 512)
        p.emit("scalar", lambda e, sl=sl: e.activation(sin_t[:, sl], tb[:], AF.Sin),
               reads=[T["tb"]], writes=[T["sin"]])
        p.emit("scalar", lambda e: e.activation(kf[:], tb[:], AF.Abs),
               reads=[T["tb"]], writes=[T["kf"]])
        p.emit("scalar", lambda e, sl=sl: e.activation(cos_t[:, sl], kf[:], AF.Sin,
                                                       scale=-1.0, bias=halfpi[:]),
               reads=[T["kf"], T["halfpi"]], writes=[T["cos"]])

    def rope_ln(src_dram, dst_sb, t_dst):
        for ch in range(NT):
            sl = slice(ch * 512, (ch + 1) * 512)
            p.dma("qload", qraw[:], src_dram[:, sl], writes=[T["qraw"]])
            p.dma("qload", qshift[0:64, :], src_dram[64:128, sl], writes=[T["qshift"]])
            p.dma("qload", qshift[64:128, :], src_dram[0:64, sl], writes=[T["qshift"]])
            p.emit("vector", lambda e, sl=sl: e.tensor_tensor(
                rw[0][0:64, :], qraw[0:64, :], cos_t[0:64, sl], ALU.mult),
                reads=[T["qraw"], T["cos"]], writes=[T["rw0"]])
            p.emit("vector", lambda e, sl=sl: e.tensor_tensor(
                rw[2][0:64, :], qshift[0:64, :], sin_t[0:64, sl], ALU.mult),
                reads=[T["qshift"], T["sin"]], writes=[T["rw2"]])
            p.emit("vector", lambda e: e.tensor_tensor(
                rw[0][0:64, :], rw[0][0:64, :], rw[2][0:64, :], ALU.subtract),
                reads=[T["rw0"], T["rw2"]], writes=[T["rw0"]])
            p.emit("vector", lambda e, sl=sl: e.tensor_tensor(
                rw[2][64:128, :], qshift[64:128, :], sin_t[64:128, sl], ALU.mult),
                reads=[T["qshift"], T["sin"]], writes=[T["rw2"]])
            p.emit("vector", lambda e, sl=sl: e.tensor_tensor(
                rw[1][64:128, :], qraw[64:128, :], cos_t[64:128, sl], ALU.mult),
                reads=[T["qraw"], T["cos"]], writes=[T["rw1"]])
            p.emit("vector", lambda e: e.tensor_tensor(
                rw[0][64:128, :], rw[1][64:128, :], rw[2][64:128, :], ALU.add),
                reads=[T["rw1"], T["rw2"]], writes=[T["rw0"]])
            p.emit("scalar", lambda e: e.copy(q_bf[:], rw[0][:]),
                   reads=[T["rw0"]], writes=[T["q_bf"]])
            p.emit("scalar", lambda e: e.activation(q2_bf[:], rw[0][:], AF.Square),
                   reads=[T["rw0"]], writes=[T["q2_bf"]])
            p.emit("tensor", lambda e: e.matmul(ps_mean[:], onesm[:], q_bf[:],
                                                start=True, stop=True),
                   reads=[T["q_bf"], T["onesm"]], writes=[T["ps_mean"]])
            p.emit("tensor", lambda e: e.matmul(ps_e2[:], onesm[:], q2_bf[:],
                                                start=True, stop=True),
                   reads=[T["q2_bf"], T["onesm"]], writes=[T["ps_e2"]])
            emit_rstd(p, r_sb, T["r"], ps_mean, ps_e2, T["ps_mean"], T["ps_e2"],
                      m2_sb, T["m2"], var_sb, T["var"], eps_col, T["eps"])
            p.emit("vector", lambda e: e.tensor_tensor(
                rw[1][:], rw[0][:], ps_mean[:], ALU.subtract),
                reads=[T["rw0"], T["ps_mean"]], writes=[T["rw1"]])
            p.emit("vector", lambda e: e.tensor_tensor(
                rw[1][:], rw[1][:], r_sb[:], ALU.mult),
                reads=[T["rw1"], T["r"]], writes=[T["rw1"]])
            p.emit("vector", lambda e, sl=sl: e.tensor_scalar(
                dst_sb[:, sl], rw[1][:], sc_col[:], bi_col[:], ALU.mult, ALU.add),
                reads=[T["rw1"], T["sc"], T["bi"]], writes=[t_dst])

    for bh in range(nbh):
        p.dma("vload", Vb[:], v_in[bh, :, :].rearrange("(t p) d -> p t d", p=128),
              writes=[T["Vb"]])
        rope_ln(qT_in[bh], Qb, T["Qb"])
        rope_ln(kT_in[bh], Kb, T["Kb"])

        for nt in range(NT):
            n0 = nt * 512
            n_m = n0 // 128 + 4
            p.emit("vector", lambda e: e.memset(dacc[:], 0.0), writes=[T["dacc"]])
            pO = nt % 2
            for mi in range(n_m):
                m0 = mi * 128
                pS = mi % 2
                kdiag = (m0 - n0) // 128
                p.emit("tensor", lambda e, m0=m0, n0=n0, pS=pS: e.matmul(
                    ps_S[pS][:], Kb[:, m0:m0 + 128], Qb[:, n0:n0 + 512],
                    start=True, stop=True),
                    reads=[T["Kb"], T["Qb"]], writes=[T[f"ps_S{pS}"]])
                p.emit("scalar", lambda e, pS=pS: e.activation(
                    S2[pS][:], ps_S[pS][:], AF.Square),
                    reads=[T[f"ps_S{pS}"]], writes=[T[f"S2_{pS}"]])
                p.emit("vector", lambda e, pS=pS: e.tensor_tensor(
                    A_bf[pS][:], S2[pS][:], S2[pS][:], ALU.mult),
                    reads=[T[f"S2_{pS}"]], writes=[T[f"A{pS}"]])
                if kdiag >= 0:
                    p.emit("vector", lambda e, pS=pS, k=kdiag: e.tensor_tensor(
                        A_bf[pS][:], A_bf[pS][:], mask_sb[:, k, :], ALU.mult),
                        reads=[T[f"A{pS}"], T["mask"]], writes=[T[f"A{pS}"]])
                p.emit("vector", lambda e, pS=pS: e.tensor_tensor(
                    dacc[:], dacc[:], A_bf[pS][:], ALU.add),
                    reads=[T["dacc"], T[f"A{pS}"]], writes=[T["dacc"]])
                p.emit("tensor", lambda e, mi=mi, pS=pS, pO=pO, first=(mi == 0),
                       last=(mi == n_m - 1): e.matmul(
                    ps_O[pO][:], Vb[:, mi, :], A_bf[pS][:],
                    start=first, stop=last),
                    reads=[T["Vb"], T[f"A{pS}"]], writes=[T[f"ps_O{pO}"]])
            p.emit("tensor", lambda e: e.matmul(ps_den[:], onesf[:], dacc[:],
                                                start=True, stop=True),
                   reads=[T["dacc"], T["onesf"]], writes=[T["ps_den"]])
            p.emit("vector", lambda e: e.reciprocal(rden[:], ps_den[:]),
                   reads=[T["ps_den"]], writes=[T["rden"]])
            p.emit("vector", lambda e, n0=n0, pO=pO: e.tensor_tensor(
                Ob[:, n0:n0 + 512], ps_O[pO][:], rden[:], ALU.mult),
                reads=[T[f"ps_O{pO}"], T["rden"]], writes=[T["Ob"]])
        p.dma("store", o_out[bh, :, :], Ob[:], reads=[T["Ob"]])

    p.final_wait("sync")
    block = st.enter_context(nc.Block())
    p.build(block)
    st.close()
    return nc



def build_launch_C(n_tok_tiles=TPC // 512):
    nc = bass.Bass("TRN2", target_bir_lowering=False)
    NTT = n_tok_tiles
    TP = NTT * 512
    EC = E // 128
    HC = HID // 128
    F1C = 2 * E // 128

    xT = nc.dram_tensor("xT", [E, TP], F32, kind="ExternalInput").ap()
    xnT = nc.dram_tensor("xnT", [E, TP], BF16, kind="ExternalInput").ap()
    ot16 = nc.dram_tensor("ot16", [H, 128, TP], BF16, kind="ExternalInput").ap()
    w_out = nc.dram_tensor("w_out", [E, E], F32, kind="ExternalInput").ap()
    w_gateT = nc.dram_tensor("w_gateT", [E, HID], F32, kind="ExternalInput").ap()
    w_upT = nc.dram_tensor("w_upT", [E, HID], F32, kind="ExternalInput").ap()
    w_downT = nc.dram_tensor("w_downT", [HID, E], F32, kind="ExternalInput").ap()
    W_inT = nc.dram_tensor("W_inT", [E, 2 * E], F32, kind="ExternalInput").ap()
    W_outT = nc.dram_tensor("W_outT", [2 * E, E], F32, kind="ExternalInput").ap()
    BC_flat = nc.dram_tensor("BC_flat", [8 * E, 16], F32, kind="ExternalInput").ap()
    norm_w = nc.dram_tensor("norm_w", [128, E // 128], F32, kind="ExternalInput").ap()
    W_baseT = nc.dram_tensor("W_baseT", [E, E], F32, kind="ExternalInput").ap()
    b_base = nc.dram_tensor("b_base", [128, E // 128], F32, kind="ExternalInput").ap()
    router_WT = nc.dram_tensor("router_WT", [E, 8], F32, kind="ExternalInput").ap()
    A_flatT = nc.dram_tensor("A_flatT", [E, 128], F32, kind="ExternalInput").ap()
    BwT = nc.dram_tensor("BwT", [128, E], F32, kind="ExternalInput").ap()
    E8 = nc.dram_tensor("E8", [8, 128], F32, kind="ExternalInput").ap()
    ident = nc.dram_tensor("ident", [128, 128], F32, kind="ExternalInput").ap()
    x2T_out = nc.dram_tensor("x2T", [E, TP], F32, kind="ExternalOutput").ap()
    x1_spill = nc.dram_tensor("x1_spill", [E, TP], F32, kind="Internal").ap()

    st = contextlib.ExitStack()
    rwT_sb = st.enter_context(nc.sbuf_tensor("rwT_sb", [128, EC, 8], BF16))
    af_sb = st.enter_context(nc.sbuf_tensor("af_sb", [128, EC, 128], BF16))
    bw_sb = st.enter_context(nc.sbuf_tensor("bw_sb", [128, EC, 128], BF16))
    e8_sb = st.enter_context(nc.sbuf_tensor("e8_sb", [8, 128], F32))
    id_sb = st.enter_context(nc.sbuf_tensor("id_sb", [128, 128], F32))
    nw_sb = st.enter_context(nc.sbuf_tensor("nw_sb", [128, EC], F32))
    bb_sb = st.enter_context(nc.sbuf_tensor("bb_sb", [128, EC], F32))
    G_sb = st.enter_context(nc.sbuf_tensor("G_sb", [128, F1C], F32))
    G2_sb = st.enter_context(nc.sbuf_tensor("G2_sb", [128, F1C], F32))
    gtmp = st.enter_context(nc.sbuf_tensor("gtmp", [128, 16], F32))
    bc_sb = st.enter_context(nc.sbuf_tensor("bc_sb", [128, 4 * F1C, 16], F32))
    onesm = st.enter_context(nc.sbuf_tensor("onesm", [128, 128], BF16))
    epsL = st.enter_context(nc.sbuf_tensor("epsL", [128, 1], F32))
    epsR = st.enter_context(nc.sbuf_tensor("epsR", [128, 1], F32))
    big = st.enter_context(nc.sbuf_tensor("big", [128, HC, 512], BF16))
    xn_sb = st.enter_context(nc.sbuf_tensor("xn_sb", [128, EC, 512], BF16))
    s_sb = st.enter_context(nc.sbuf_tensor("s_sb", [128, EC, 512], BF16))
    g_sb = st.enter_context(nc.sbuf_tensor("g_sb", [128, EC, 512], BF16))
    ws = [st.enter_context(nc.sbuf_tensor(f"ws{i}", [128, F1C, 128], BF16)) for i in range(2)]
    wd = [st.enter_context(nc.sbuf_tensor(f"wd{i}", [128, HC, 128], BF16)) for i in range(2)]
    x_ch = st.enter_context(nc.sbuf_tensor("x_ch", [128, 512], F32))
    x1_ch = st.enter_context(nc.sbuf_tensor("x1_ch", [128, 512], F32))
    m2_sb = st.enter_context(nc.sbuf_tensor("m2_sb", [128, 512], F32))
    var_sb = st.enter_context(nc.sbuf_tensor("var_sb", [128, 512], F32))
    r_sb = st.enter_context(nc.sbuf_tensor("r_sb", [128, 512], F32))
    t_f = st.enter_context(nc.sbuf_tensor("t_f", [128, 512], F32))
    u_f = st.enter_context(nc.sbuf_tensor("u_f", [128, 512], F32))
    silu_sb = st.enter_context(nc.sbuf_tensor("silu_sb", [128, 512], F32))
    b_bf = st.enter_context(nc.sbuf_tensor("b_bf", [128, 512], BF16))
    logit_sb = st.enter_context(nc.sbuf_tensor("logit_sb", [128, 4, 8], F32))
    mx_sb = st.enter_context(nc.sbuf_tensor("mx_sb", [128, 4], F32))
    oh_sb = st.enter_context(nc.sbuf_tensor("oh_sb", [128, 4, 8], F32))
    ohT_sb = st.enter_context(nc.sbuf_tensor("ohT_sb", [8, 512], F32))
    maskT_sb = st.enter_context(nc.sbuf_tensor("maskT_sb", [128, 512], BF16))
    hm_bf = st.enter_context(nc.sbuf_tensor("hm_bf", [128, 512], BF16))
    ps_a = st.enter_context(nc.psum_tensor("ps_a", [128, 512], F32))
    ps_b = st.enter_context(nc.psum_tensor("ps_b", [128, 512], F32))
    ps_c = st.enter_context(nc.psum_tensor("ps_c", [128, 512], F32))
    ps_d = st.enter_context(nc.psum_tensor("ps_d", [128, 512], F32))
    ps_mean = st.enter_context(nc.psum_tensor("ps_mean", [128, 512], F32))
    ps_e2 = st.enter_context(nc.psum_tensor("ps_e2", [128, 512], F32))
    ps_t8 = st.enter_context(nc.psum_tensor("ps_t8", [128, 512], F32))

    p = Prog(nc, st)
    p.stream("aload", "sync")
    p.stream("bload", "sync")
    p.stream("wl0", "gpsimd")
    p.stream("wl1", "gpsimd")
    p.stream("cload", "gpsimd")
    p.stream("store", "scalar")
    p.stream("spill", "scalar")

    tok = {}

    def TK(name):
        if name not in tok:
            tok[name] = Tok(name)
        return tok[name]

    p.dma("cload", rwT_sb[:], router_WT.rearrange("(c p) f -> p c f", p=128), writes=[TK("rwT")])
    p.dma("cload", af_sb[:], A_flatT.rearrange("(c p) f -> p c f", p=128), writes=[TK("af")])
    p.dma("cload", bw_sb[:], BwT.rearrange("p (c f) -> p c f", f=128), writes=[TK("bw")])
    p.dma("aload", e8_sb[:], E8[:, :], writes=[TK("e8")])
    p.dma("aload", id_sb[:], ident[:, :], writes=[TK("id")])
    p.dma("aload", nw_sb[:], norm_w[:, :], writes=[TK("nw")])
    p.dma("aload", bb_sb[:], b_base[:, :], writes=[TK("bb")])
    p.dma("aload", bc_sb[:], BC_flat.rearrange("(c p) s -> p c s", p=128), writes=[TK("bc")])
    p.emit("vector", lambda e: e.memset(onesm[:], 1.0 / E), writes=[TK("onesm")])
    p.emit("vector", lambda e: e.memset(epsL[:], LN_EPS), writes=[TK("epsL")])
    p.emit("vector", lambda e: e.memset(epsR[:], RMS_EPS), writes=[TK("epsR")])

    for c in range(F1C):
        p.emit("vector", lambda e, c=c: e.scalar_tensor_tensor(
            gtmp[:], bc_sb[:, c, :], 1.0, bc_sb[:, F1C + c, :],
            ALU.mult, ALU.mult, accum_out=G_sb[:, c:c + 1]),
            reads=[TK("bc")], writes=[TK("G"), TK("gtmp")])
    for c in range(F1C):
        p.emit("vector", lambda e, c=c: e.scalar_tensor_tensor(
            gtmp[:], bc_sb[:, 2 * F1C + c, :], 1.0, bc_sb[:, 3 * F1C + c, :],
            ALU.mult, ALU.mult, accum_out=G2_sb[:, c:c + 1]),
            reads=[TK("bc")], writes=[TK("G2"), TK("gtmp")])
    p.emit("vector", lambda e: e.tensor_tensor(G_sb[:], G_sb[:], G2_sb[:], ALU.add),
           reads=[TK("G"), TK("G2")], writes=[TK("G")])

    def colstrip(dram, j, nch):
        return dram[:, j * 128:(j + 1) * 128].rearrange("(c p) f -> p c f", p=128)

    for Ti in range(NTT):
        c0 = Ti * 512
        csl = slice(c0, c0 + 512)
        p.dma("aload", xn_sb[:], xnT[:, csl].rearrange("(c p) f -> p c f", p=128),
              writes=[TK("xn")])
        p.dma("aload", big[:, 0:16, :], ot16[:, :, csl].rearrange("h p f -> p h f"),
              writes=[TK("big")])

        for eo in range(EC):
            b = eo % 2
            pst, pt = (ps_a, "ps_a") if b == 0 else (ps_b, "ps_b")
            p.dma(f"wl{b}", ws[b][:, 0:16, :], colstrip(w_out, eo, EC), writes=[TK(f"ws{b}")])
            for h in range(H):
                p.emit("tensor", lambda e, h=h, b=b, pst=pst: e.matmul(
                    pst[:], ws[b][:, h, :], big[:, h, :],
                    start=(h == 0), stop=(h == H - 1)),
                    reads=[TK(f"ws{b}"), TK("big")], writes=[TK(pt)])
            p.emit("vector", lambda e, eo=eo, pst=pst: e.tensor_copy(s_sb[:, eo, :], pst[:]),
                   reads=[TK(pt)], writes=[TK("s")])

        for hc in range(HC):
            b = hc % 2
            p.dma(f"wl{b}", ws[b][:, 0:16, :], colstrip(w_gateT, hc, EC), writes=[TK(f"ws{b}")])
            p.dma(f"wl{b}", ws[b][:, 16:32, :], colstrip(w_upT, hc, EC), writes=[TK(f"ws{b}")])
            for c in range(EC):
                p.emit("tensor", lambda e, b=b, c=c: e.matmul(
                    ps_c[:], ws[b][:, c, :], xn_sb[:, c, :],
                    start=(c == 0), stop=(c == EC - 1)),
                    reads=[TK(f"ws{b}"), TK("xn")], writes=[TK("ps_c")])
            for c in range(EC):
                p.emit("tensor", lambda e, b=b, c=c: e.matmul(
                    ps_d[:], ws[b][:, 16 + c, :], xn_sb[:, c, :],
                    start=(c == 0), stop=(c == EC - 1)),
                    reads=[TK(f"ws{b}"), TK("xn")], writes=[TK("ps_d")])
            p.emit("scalar", lambda e: e.activation(silu_sb[:], ps_c[:], AF.Silu),
                   reads=[TK("ps_c")], writes=[TK("silu")])
            p.emit("vector", lambda e, hc=hc: e.tensor_tensor(
                big[:, hc, :], ps_d[:], silu_sb[:], ALU.mult),
                reads=[TK("ps_d"), TK("silu")], writes=[TK("big")])

        for eo in range(EC):
            b = eo % 2
            p.dma(f"wl{b}", wd[b][:], colstrip(w_downT, eo, HC), writes=[TK(f"wd{b}")])
            for hc in range(HC):
                p.emit("tensor", lambda e, b=b, hc=hc: e.matmul(
                    ps_c[:], wd[b][:, hc, :], big[:, hc, :],
                    start=(hc == 0), stop=(hc == HC - 1)),
                    reads=[TK(f"wd{b}"), TK("big")], writes=[TK("ps_c")])
            p.emit("scalar", lambda e, eo=eo: e.activation(g_sb[:, eo, :], ps_c[:], AF.Silu),
                   reads=[TK("ps_c")], writes=[TK("g")])
        for c in range(EC):
            p.dma("bload", x_ch[:], xT[c * 128:(c + 1) * 128, csl], writes=[TK("x_ch")])
            p.emit("vector", lambda e, c=c: e.tensor_tensor(
                t_f[:], s_sb[:, c, :], g_sb[:, c, :], ALU.mult),
                reads=[TK("s"), TK("g")], writes=[TK("t_f")])
            p.emit("vector", lambda e, c=c: e.tensor_tensor(
                x1_ch[:], x_ch[:], t_f[:], ALU.add),
                reads=[TK("x_ch"), TK("t_f")], writes=[TK("x1_ch")])
            p.emit("scalar", lambda e, c=c: e.copy(b_bf[:], x1_ch[:]),
                   reads=[TK("x1_ch")], writes=[TK("b_bf")])
            p.emit("scalar", lambda e, c=c: e.activation(maskT_sb[:], x1_ch[:], AF.Square),
                   reads=[TK("x1_ch")], writes=[TK("maskT")])
            p.emit("tensor", lambda e, c=c: e.matmul(ps_mean[:], onesm[:], b_bf[:],
                                                     start=(c == 0), stop=(c == EC - 1)),
                   reads=[TK("b_bf"), TK("onesm")], writes=[TK("ps_mean")])
            p.emit("tensor", lambda e, c=c: e.matmul(ps_e2[:], onesm[:], maskT_sb[:],
                                                     start=(c == 0), stop=(c == EC - 1)),
                   reads=[TK("maskT"), TK("onesm")], writes=[TK("ps_e2")])
            p.dma("spill", x1_spill[c * 128:(c + 1) * 128, csl], x1_ch[:],
                  reads=[TK("x1_ch")])
        emit_rstd(p, r_sb, TK("r"), ps_mean, ps_e2, TK("ps_mean"), TK("ps_e2"),
                  m2_sb, TK("m2"), var_sb, TK("var"), epsL, TK("epsL"))
        for c in range(EC):
            p.dma("bload", x1_ch[:], x1_spill[c * 128:(c + 1) * 128, csl],
                  writes=[TK("x1_ch")])
            p.emit("vector", lambda e, c=c: e.tensor_tensor(
                t_f[:], x1_ch[:], ps_mean[:], ALU.subtract),
                reads=[TK("x1_ch"), TK("ps_mean")], writes=[TK("t_f")])
            p.emit("vector", lambda e, c=c: e.tensor_tensor(
                xn_sb[:, c, :], t_f[:], r_sb[:], ALU.mult),
                reads=[TK("t_f"), TK("r")], writes=[TK("xn")])

        for f in range(F1C):
            b = f % 2
            p.dma(f"wl{b}", ws[b][:, 0:16, :], colstrip(W_inT, f, EC), writes=[TK(f"ws{b}")])
            for c in range(EC):
                p.emit("tensor", lambda e, b=b, c=c: e.matmul(
                    ps_c[:], ws[b][:, c, :], xn_sb[:, c, :],
                    start=(c == 0), stop=(c == EC - 1)),
                    reads=[TK(f"ws{b}"), TK("xn")], writes=[TK("ps_c")])
            p.emit("vector", lambda e, f=f: e.tensor_scalar(
                big[:, f, :], ps_c[:], G_sb[:, f:f + 1], None, ALU.mult),
                reads=[TK("ps_c"), TK("G")], writes=[TK("big")])
        for eo in range(EC):
            b = eo % 2
            p.dma(f"wl{b}", ws[b][:], colstrip(W_outT, eo, F1C), writes=[TK(f"ws{b}")])
            for f in range(F1C):
                p.emit("tensor", lambda e, b=b, f=f: e.matmul(
                    ps_d[:], ws[b][:, f, :], big[:, f, :],
                    start=(f == 0), stop=(f == F1C - 1)),
                    reads=[TK(f"ws{b}"), TK("big")], writes=[TK("ps_d")])
            p.emit("vector", lambda e, eo=eo: e.tensor_copy(g_sb[:, eo, :], ps_d[:]),
                   reads=[TK("ps_d")], writes=[TK("g")])
        for c in range(EC):
            p.emit("scalar", lambda e, c=c: e.activation(b_bf[:], g_sb[:, c, :], AF.Square),
                   reads=[TK("g")], writes=[TK("b_bf")])
            p.emit("tensor", lambda e, c=c: e.matmul(ps_e2[:], onesm[:], b_bf[:],
                                                     start=(c == 0), stop=(c == EC - 1)),
                   reads=[TK("b_bf"), TK("onesm")], writes=[TK("ps_e2")])
        p.emit("scalar", lambda e: e.activation(var_sb[:], ps_e2[:], AF.Ln, bias=epsR[:]),
               reads=[TK("ps_e2"), TK("epsR")], writes=[TK("var")])
        p.emit("scalar", lambda e: e.activation(r_sb[:], var_sb[:], AF.Exp, scale=-0.5),
               reads=[TK("var")], writes=[TK("r")])
        for c in range(EC):
            p.emit("vector", lambda e, c=c: e.tensor_tensor(
                t_f[:], g_sb[:, c, :], r_sb[:], ALU.mult),
                reads=[TK("g"), TK("r")], writes=[TK("t_f")])
            p.emit("vector", lambda e, c=c: e.scalar_tensor_tensor(
                s_sb[:, c, :], t_f[:], nw_sb[:, c:c + 1], xn_sb[:, c, :],
                ALU.mult, ALU.add),
                reads=[TK("t_f"), TK("nw"), TK("xn")], writes=[TK("s")])

        for eo in range(EC):
            b = eo % 2
            p.dma(f"wl{b}", ws[b][:, 0:16, :], colstrip(W_baseT, eo, EC), writes=[TK(f"ws{b}")])
            for c in range(EC):
                p.emit("tensor", lambda e, b=b, c=c: e.matmul(
                    ps_c[:], ws[b][:, c, :], xn_sb[:, c, :],
                    start=(c == 0), stop=(c == EC - 1)),
                    reads=[TK(f"ws{b}"), TK("xn")], writes=[TK("ps_c")])
            p.emit("vector", lambda e, eo=eo: e.tensor_scalar(
                g_sb[:, eo, :], ps_c[:], bb_sb[:, eo:eo + 1], None, ALU.add),
                reads=[TK("ps_c"), TK("bb")], writes=[TK("g")])
        for tsub in range(4):
            for c in range(EC):
                p.emit("tensor", lambda e, c=c, tsub=tsub: e.matmul(
                    ps_t8[0:128, 0:8], xn_sb[:, c, tsub * 128:(tsub + 1) * 128],
                    rwT_sb[:, c, :], start=(c == 0), stop=(c == EC - 1)),
                    reads=[TK("xn"), TK("rwT")], writes=[TK("ps_t8")])
            p.emit("vector", lambda e, tsub=tsub: e.tensor_copy(
                logit_sb[:, tsub, :], ps_t8[0:128, 0:8]),
                reads=[TK("ps_t8")], writes=[TK("logit")])
        p.emit("vector", lambda e: e.tensor_reduce(
            mx_sb[:], logit_sb[:], mybir.AxisListType.X, ALU.max),
            reads=[TK("logit")], writes=[TK("mx")])
        for tsub in range(4):
            p.emit("vector", lambda e, tsub=tsub: e.tensor_scalar(
                oh_sb[:, tsub, :], logit_sb[:, tsub, :], mx_sb[:, tsub:tsub + 1],
                None, ALU.is_ge),
                reads=[TK("logit"), TK("mx")], writes=[TK("oh")])
            p.emit("tensor", lambda e, tsub=tsub: e.transpose(
                ps_t8[0:8, 0:128], oh_sb[:, tsub, :], id_sb[:]),
                reads=[TK("oh"), TK("id")], writes=[TK("ps_t8")])
            p.emit("vector", lambda e, tsub=tsub: e.tensor_copy(
                ohT_sb[:, tsub * 128:(tsub + 1) * 128], ps_t8[0:8, 0:128]),
                reads=[TK("ps_t8")], writes=[TK("ohT")])
        p.emit("tensor", lambda e: e.matmul(ps_t8[:], e8_sb[:], ohT_sb[:],
                                            start=True, stop=True),
               reads=[TK("e8"), TK("ohT")], writes=[TK("ps_t8")])
        p.emit("vector", lambda e: e.tensor_copy(maskT_sb[:], ps_t8[:]),
               reads=[TK("ps_t8")], writes=[TK("maskT")])
        for c in range(EC):
            p.emit("tensor", lambda e, c=c: e.matmul(
                ps_c[:], af_sb[:, c, :], xn_sb[:, c, :],
                start=(c == 0), stop=(c == EC - 1)),
                reads=[TK("af"), TK("xn")], writes=[TK("ps_c")])
        p.emit("vector", lambda e: e.tensor_tensor(hm_bf[:], ps_c[:], maskT_sb[:], ALU.mult),
               reads=[TK("ps_c"), TK("maskT")], writes=[TK("hm")])
        for eo in range(EC):
            p.emit("tensor", lambda e, eo=eo: e.matmul(
                ps_d[:], bw_sb[:, eo, :], hm_bf[:], start=True, stop=True),
                reads=[TK("bw"), TK("hm")], writes=[TK("ps_d")])
            p.emit("vector", lambda e, eo=eo: e.scalar_tensor_tensor(
                t_f[:], ps_d[:], 2.0, g_sb[:, eo, :], ALU.mult, ALU.add),
                reads=[TK("ps_d"), TK("g")], writes=[TK("t_f")])
            p.emit("scalar", lambda e: e.activation(u_f[:], t_f[:], AF.Silu),
                   reads=[TK("t_f")], writes=[TK("u_f")])
            p.emit("vector", lambda e, eo=eo: e.tensor_tensor(
                u_f[:], u_f[:], s_sb[:, eo, :], ALU.mult),
                reads=[TK("u_f"), TK("s")], writes=[TK("u_f")])
            p.dma("bload", x1_ch[:], x1_spill[eo * 128:(eo + 1) * 128, csl],
                  writes=[TK("x1_ch")])
            p.emit("vector", lambda e, eo=eo: e.tensor_tensor(
                x1_ch[:], x1_ch[:], u_f[:], ALU.add),
                reads=[TK("x1_ch"), TK("u_f")], writes=[TK("x1_ch")])
            p.dma("store", x2T_out[eo * 128:(eo + 1) * 128, csl], x1_ch[:],
                  reads=[TK("x1_ch")])

    p.final_wait("sync")
    block = st.enter_context(nc.Block())
    p.build(block)
    st.close()
    return nc



def _bf(x):
    return np.ascontiguousarray(x, dtype=BF)


def kernel(L0_attn_x, L0_attn_w_qkv, L0_attn_w_out, L0_attn_ln_scale,
           L0_attn_ln_bias, L0_ffn_w_gate, L0_ffn_w_up, L0_ffn_w_down,
           L1_attn_W_in, L1_attn_W_out, L1_attn_A_theta, L1_attn_B_real,
           L1_attn_B_imag, L1_attn_C_real, L1_attn_C_imag, L1_attn_norm_w,
           L1_ffn_W_base, L1_ffn_b_base, L1_ffn_router_W, L1_ffn_A, L1_ffn_Bw):
    x = np.asarray(L0_attn_x, dtype=np.float32)
    w_qkv = np.asarray(L0_attn_w_qkv, np.float32)

    ncA = build_launch_A()
    in_A = []
    for c in range(NCORE):
        b, r0 = divmod(c, 4)
        xs = x[b, r0 * TPC:(r0 + 1) * TPC, :]
        in_A.append({"xT": np.ascontiguousarray(xs.T), "w_qkv": w_qkv})
    resA = run_bass_kernel_spmd(ncA, in_A, core_ids=list(range(NCORE)))
    xnT = [resA.results[c]["xnT"] for c in range(NCORE)]
    qkvT = [resA.results[c]["qkvT"] for c in range(NCORE)]

    qkvT_b = [np.concatenate([qkvT[4 * b + i] for i in range(4)], axis=1) for b in range(B)]

    ncB = build_launch_B()
    masks = np.zeros((4, 128, 512), BF)
    for k in range(4):
        pp, ff = np.meshgrid(np.arange(128), np.arange(512), indexing="ij")
        masks[k] = (ff >= pp + k * 128).astype(BF)
    in_B = []
    for c in range(NCORE):
        b, hg = divmod(c, 4)
        hs = [4 * hg + i for i in range(4)]
        qT = np.stack([qkvT_b[b][h * D:(h + 1) * D, :] for h in hs])
        kT = np.stack([qkvT_b[b][E + h * D:E + (h + 1) * D, :] for h in hs])
        vT = np.stack([qkvT_b[b][2 * E + h * D:2 * E + (h + 1) * D, :] for h in hs])
        v = np.ascontiguousarray(vT.transpose(0, 2, 1))
        in_B.append({
            "qT": np.ascontiguousarray(qT), "kT": np.ascontiguousarray(kT), "v": v,
            "lnsc": np.ascontiguousarray(np.asarray(L0_attn_ln_scale, np.float32).reshape(128, 1)),
            "lnbi": np.ascontiguousarray(np.asarray(L0_attn_ln_bias, np.float32).reshape(128, 1)),
            "masks": masks,
        })
    resB = run_bass_kernel_spmd(ncB, in_B, core_ids=list(range(NCORE)))
    o_b = []
    for b in range(B):
        o_b.append(np.concatenate([resB.results[4 * b + i]["o"] for i in range(4)], axis=0))

    ncC = build_launch_C()
    Br = np.asarray(L1_attn_B_real, np.float32).reshape(2 * E, 16)
    Bi = np.asarray(L1_attn_B_imag, np.float32).reshape(2 * E, 16)
    Cr = np.asarray(L1_attn_C_real, np.float32).reshape(2 * E, 16)
    Ci = np.asarray(L1_attn_C_imag, np.float32).reshape(2 * E, 16)
    BC_flat = np.ascontiguousarray(np.concatenate([Br, Cr, Bi, -Ci], 0))
    A_flatT = np.ascontiguousarray(
        np.asarray(L1_ffn_A, np.float32).reshape(128, E).T)
    BwT = np.ascontiguousarray(
        np.asarray(L1_ffn_Bw, np.float32).transpose(0, 2, 1).reshape(128, E))
    in_C = []
    for c in range(NCORE):
        b, r0 = divmod(c, 4)
        xs = x[b, r0 * TPC:(r0 + 1) * TPC, :]
        in_C.append({
            "xT": np.ascontiguousarray(xs.T),
            "xnT": xnT[c],
            "ot16": np.ascontiguousarray(o_b[b][:, :, r0 * TPC:(r0 + 1) * TPC]),
            "w_out": np.asarray(L0_attn_w_out, np.float32),
            "w_gateT": np.ascontiguousarray(np.asarray(L0_ffn_w_gate, np.float32).T),
            "w_upT": np.ascontiguousarray(np.asarray(L0_ffn_w_up, np.float32).T),
            "w_downT": np.ascontiguousarray(np.asarray(L0_ffn_w_down, np.float32).T),
            "W_inT": np.ascontiguousarray(np.asarray(L1_attn_W_in, np.float32).T),
            "W_outT": np.ascontiguousarray(np.asarray(L1_attn_W_out, np.float32).T),
            "BC_flat": BC_flat,
            "norm_w": np.ascontiguousarray(np.asarray(L1_attn_norm_w, np.float32).reshape(16, 128).T),
            "W_baseT": np.ascontiguousarray(np.asarray(L1_ffn_W_base, np.float32).T),
            "b_base": np.ascontiguousarray(np.asarray(L1_ffn_b_base, np.float32).reshape(16, 128).T),
            "router_WT": np.ascontiguousarray(np.asarray(L1_ffn_router_W, np.float32).T),
            "A_flatT": A_flatT,
            "BwT": BwT,
            "E8": np.ascontiguousarray(np.repeat(np.eye(8, dtype=np.float32), 16, axis=1)),
            "ident": np.eye(128, dtype=np.float32),
        })
    resC = run_bass_kernel_spmd(ncC, in_C, core_ids=list(range(NCORE)))

    out = np.empty((B, S, E), np.float32)
    for c in range(NCORE):
        b, r0 = divmod(c, 4)
        out[b, r0 * TPC:(r0 + 1) * TPC, :] = resC.results[c]["x2T"].T
    return out
